# version 05
# baseline (speedup 1.0000x reference)
import numpy as np
from contextlib import ExitStack
import concourse.bass as bass
import concourse.mybir as mybir
from concourse.bass_utils import run_bass_kernel_spmd

F32 = mybir.dt.float32
BF16 = mybir.dt.bfloat16
AF = mybir.ActivationFunctionType
ALU = mybir.AluOpType
AX = mybir.AxisListType

D = 1024
T = 512
SUBS = 4
NQ = 8
CPQ = 16
EPS = 1e-5
NGP = 8
ODV = 6
NDV = 16 - NGP
SU_BASE = 8


def build_core_program(L=8192):
    NT = L // T
    nc = bass.Bass("TRN2", target_bir_lowering=False)

    x_d = nc.dram_tensor("x", [L, D], F32, kind="ExternalInput").ap()
    wq_d = nc.dram_tensor("wq", [D, D], F32, kind="ExternalInput").ap()
    wk_d = nc.dram_tensor("wk", [D, D], F32, kind="ExternalInput").ap()
    wv_d = nc.dram_tensor("wv", [D, D], F32, kind="ExternalInput").ap()
    rm_d = nc.dram_tensor("rm", [4, 64, 16, 16], F32, kind="ExternalInput").ap()
    rup_d = nc.dram_tensor("rup", [4, 64, 64, 16], F32, kind="ExternalInput").ap()
    wc_d = nc.dram_tensor("wc", [D, 16384], F32, kind="ExternalInput").ap()
    out_d = nc.dram_tensor("out", [L, D], F32, kind="ExternalOutput").ap()

    wq_bf = nc.dram_tensor("wq_bf", [D, D], BF16).ap()
    wk_bf = nc.dram_tensor("wk_bf", [D, D], BF16).ap()
    wv_bf = nc.dram_tensor("wv_bf", [D, D], BF16).ap()
    wc_bf = nc.dram_tensor("wc_bf", [D, 16384], BF16).ap()
    wct = nc.dram_tensor("wct", [NQ, 2, 2, 128, 8, 512], BF16).ap()
    zdram = nc.dram_tensor("zdram", [32, 4, 32, 128], BF16).ap()

    with ExitStack() as ES:
        wqt = ES.enter_context(nc.sbuf_tensor("wqt", [128, 8, D], BF16))
        wkt = ES.enter_context(nc.sbuf_tensor("wkt", [128, 8, D], BF16))
        wvt = ES.enter_context(nc.sbuf_tensor("wvt", [128, 8, D], BF16))
        zstat = ES.enter_context(nc.sbuf_tensor("zstat", [128, 32, 128], BF16))
        idb = ES.enter_context(nc.sbuf_tensor("idb", [128, 128], BF16))
        msk = ES.enter_context(nc.sbuf_tensor("msk", [32, 2], F32))
        x_sb = ES.enter_context(nc.sbuf_tensor("x_sb", [128, SUBS, D], F32))
        xn_sb = ES.enter_context(nc.sbuf_tensor("xn_sb", [128, 2, D], BF16))
        xnt = ES.enter_context(nc.sbuf_tensor("xnt", [128, 8, T], BF16))
        eQb = ES.enter_context(nc.sbuf_tensor("eQb", [128, 2, D], BF16))
        eKb = ES.enter_context(nc.sbuf_tensor("eKb", [128, 4, D], BF16))
        eQg = ES.enter_context(nc.sbuf_tensor("eQg", [128, 2, D], BF16))
        Vb = ES.enter_context(nc.sbuf_tensor("Vb", [128, 4, D], BF16))
        kd = ES.enter_context(nc.sbuf_tensor("kd", [128, 2, 64], F32))
        kg = ES.enter_context(nc.sbuf_tensor("kg", [128, 2, 64], F32))
        qd = ES.enter_context(nc.sbuf_tensor("qd", [128, 2, 16], F32))
        rq = ES.enter_context(nc.sbuf_tensor("rq", [128, 2, 16], F32))
        Mf = ES.enter_context(nc.sbuf_tensor("Mf", [128, 256], F32))
        Mb = ES.enter_context(nc.sbuf_tensor("Mb", [128, 2, 256], BF16))
        prodM = ES.enter_context(nc.sbuf_tensor("prodM", [128, 3, D], BF16))
        prodO = ES.enter_context(nc.sbuf_tensor("prodO", [128, 2, D], F32))
        attn = ES.enter_context(nc.sbuf_tensor("attn", [128, SUBS, D], BF16))
        hT = ES.enter_context(nc.sbuf_tensor("hT", [128, CPQ, T], BF16))
        wsb = ES.enter_context(nc.sbuf_tensor("wsb", [128, 2, 8, 512], BF16))
        partial = ES.enter_context(nc.sbuf_tensor("partial", [128, SUBS, D], F32))
        sm = ES.enter_context(nc.sbuf_tensor("sm", [128, SUBS], F32))
        sxx = ES.enter_context(nc.sbuf_tensor("sxx", [128, SUBS], F32))
        mean = ES.enter_context(nc.sbuf_tensor("mean", [128, SUBS], F32))
        ex2 = ES.enter_context(nc.sbuf_tensor("ex2", [128, SUBS], F32))
        msq = ES.enter_context(nc.sbuf_tensor("msq", [128, SUBS], F32))
        var = ES.enter_context(nc.sbuf_tensor("var", [128, SUBS], F32))
        lnv = ES.enter_context(nc.sbuf_tensor("lnv", [128, SUBS], F32))
        rstd = ES.enter_context(nc.sbuf_tensor("rstd", [128, SUBS], F32))
        nb = ES.enter_context(nc.sbuf_tensor("nb", [128, SUBS], F32))
        stage = ES.enter_context(nc.sbuf_tensor("stage", [128, 2, D], BF16))
        bdt = ES.enter_context(nc.sbuf_tensor("bdt", [32, 4, D], BF16))
        bdrm = ES.enter_context(nc.sbuf_tensor("bdrm", [32, 128], BF16))
        rupL = ES.enter_context(nc.sbuf_tensor("rupL", [64, 32, 128], BF16))
        rupS = ES.enter_context(nc.sbuf_tensor("rupS", [32, 256], BF16))
        zstg = ES.enter_context(nc.sbuf_tensor("zstg", [32, 2, 4, 128], BF16))
        cst = ES.enter_context(nc.sbuf_tensor("cst", [128, 2], F32))
        sqs = ES.enter_context(nc.sbuf_tensor("sqs", [128, D], BF16))
        pQ0 = ES.enter_context(nc.psum_tensor("pQ0", [128, 512], F32))
        pQ1 = ES.enter_context(nc.psum_tensor("pQ1", [128, 512], F32))
        pK0 = ES.enter_context(nc.psum_tensor("pK0", [128, 512], F32))
        pK1 = ES.enter_context(nc.psum_tensor("pK1", [128, 512], F32))
        pV0 = ES.enter_context(nc.psum_tensor("pV0", [128, 512], F32))
        pV1 = ES.enter_context(nc.psum_tensor("pV1", [128, 512], F32))
        pT = ES.enter_context(nc.psum_tensor("pT", [128, 512], BF16))
        pS = ES.enter_context(nc.psum_tensor("pS", [128, 512], F32))
        s_sg = ES.enter_context(nc.semaphore("s_sg"))
        s_su = ES.enter_context(nc.semaphore("s_su"))
        s_sl = ES.enter_context(nc.semaphore("s_sl"))
        s_tA = ES.enter_context(nc.semaphore("s_tA"))
        s_eA = ES.enter_context(nc.semaphore("s_eA"))
        s_cm = ES.enter_context(nc.semaphore("s_cm"))
        s_mk = ES.enter_context(nc.semaphore("s_mk"))
        s_zd0 = ES.enter_context(nc.semaphore("s_zd0"))
        s_zd1 = ES.enter_context(nc.semaphore("s_zd1"))
        s_zs = ES.enter_context(nc.semaphore("s_zs"))
        s_stg0 = ES.enter_context(nc.semaphore("s_stg0"))
        s_stg1 = ES.enter_context(nc.semaphore("s_stg1"))
        s_ss0 = ES.enter_context(nc.semaphore("s_ss0"))
        s_ss1 = ES.enter_context(nc.semaphore("s_ss1"))
        s_wqt = ES.enter_context(nc.semaphore("s_wqt"))
        s_x = ES.enter_context(nc.semaphore("s_x"))
        s_w0 = ES.enter_context(nc.semaphore("s_w0"))
        s_w1 = ES.enter_context(nc.semaphore("s_w1"))
        s_wf = ES.enter_context(nc.semaphore("s_wf"))
        s_ln = ES.enter_context(nc.semaphore("s_ln"))
        s_l2 = ES.enter_context(nc.semaphore("s_l2"))
        s_l3 = ES.enter_context(nc.semaphore("s_l3"))
        s_l4 = ES.enter_context(nc.semaphore("s_l4"))
        s_tp = ES.enter_context(nc.semaphore("s_tp"))
        s_xt = ES.enter_context(nc.semaphore("s_xt"))
        s_pq = ES.enter_context(nc.semaphore("s_pq"))
        s_pk = ES.enter_context(nc.semaphore("s_pk"))
        s_pv = ES.enter_context(nc.semaphore("s_pv"))
        s_q = ES.enter_context(nc.semaphore("s_q"))
        s_k = ES.enter_context(nc.semaphore("s_k"))
        s_v = ES.enter_context(nc.semaphore("s_v"))
        s_dg = ES.enter_context(nc.semaphore("s_dg"))
        s_gm = ES.enter_context(nc.semaphore("s_gm"))
        s_mr = ES.enter_context(nc.semaphore("s_mr"))
        s_dM = ES.enter_context(nc.semaphore("s_dM"))
        s_go = ES.enter_context(nc.semaphore("s_go"))
        s_or = ES.enter_context(nc.semaphore("s_or"))
        s_z = ES.enter_context(nc.semaphore("s_z"))
        s_gel = ES.enter_context(nc.semaphore("s_gel"))
        s_cp = ES.enter_context(nc.semaphore("s_cp"))
        s_ce = ES.enter_context(nc.semaphore("s_ce"))
        s_fin = ES.enter_context(nc.semaphore("s_fin"))
        s_vch = ES.enter_context(nc.semaphore("s_vch"))
        s_ach = ES.enter_context(nc.semaphore("s_ach"))
        s_out = ES.enter_context(nc.semaphore("s_out"))
        block = ES.enter_context(nc.Block())

        nc.const_aps.register(F32, EPS, cst[:, 0:1])
        nc.const_aps.register(F32, -0.5, cst[:, 1:2])

        vch = [0]
        ach = [0]

        def g_of(t, s):
            return t * SUBS + s

        @block.gpsimd
        def _(gp):
            gp.memset(idb[:, :], 0.0).then_inc(s_su, 1)
            gp.wait_ge(s_su, 1)
            gp.affine_select(out=idb[:, :], in_=idb[:, :], compare_op=ALU.not_equal,
                             fill=1.0, base=0, pattern=[[-1, 128]],
                             channel_multiplier=1).then_inc(s_su, 1)
            gp.memset(msk[:, :], 1.0).then_inc(s_su, 1)
            gp.wait_ge(s_su, 3)
            gp.affine_select(out=msk[:, 0:1], in_=msk[:, 0:1], compare_op=ALU.is_ge,
                             fill=0.0, base=15, pattern=[[0, 1]],
                             channel_multiplier=-1).then_inc(s_su, 1)
            gp.affine_select(out=msk[:, 1:2], in_=msk[:, 1:2], compare_op=ALU.is_ge,
                             fill=0.0, base=-16, pattern=[[0, 1]],
                             channel_multiplier=1).then_inc(s_su, 1)
            gp.memset(cst[:, 0:1], EPS).then_inc(s_su, 1)
            gp.memset(cst[:, 1:2], -0.5).then_inc(s_su, 1)
            gp.memset(bdt[:, :, :], 0.0).then_inc(s_su, 1)

            for h in range(4):
                for r2 in range(2):
                    src = rm_d[h, :, :, :].rearrange(
                        "(c p r) i j -> r i (c p) j", c=8, p=4, r=2)[r2, :, :, :]
                    dst = bdt[16 * r2:16 * r2 + 16, h, :].rearrange(
                        "i (pi q) -> i pi q", q=32)[:, :, 16 * r2:16 * r2 + 16]
                    gp.dma_start(dst, src).then_inc(s_sl, 16)
            for h in range(4):
                src = rup_d[h, :, :, :].rearrange("(c p r) k j -> k c p r j",
                                                  c=8, p=4, r=2)
                dst = rupL[:, 8 * h:8 * h + 8, :].rearrange(
                    "k c (p r j) -> k c p r j", p=4, r=2)
                gp.dma_start(dst, src).then_inc(s_sl, 16)

            sgc = [0]

            def cast_dma(dst, srcp):
                if sgc[0] > 0:
                    gp.wait_ge(s_sg, sgc[0])
                gp.dma_start(dst, srcp).then_inc(s_sg, 16)
                sgc[0] += 16

            cast_dma(wq_bf[:, :], wq_d[:, :])
            cast_dma(wk_bf[:, :], wk_d[:, :])
            cast_dma(wv_bf[:, :], wv_d[:, :])

            def wc_cast(j):
                cast_dma(wc_bf[:, j * 2048:(j + 1) * 2048],
                         wc_d[:, j * 2048:(j + 1) * 2048])
            wc_cast(0); wc_cast(1)

            for hc in range(32):
                gp.wait_ge(s_mk, hc + 1)
                gp.dma_start(zdram[hc, :, :, :].rearrange("p q c -> q p c"),
                             zstg[:, hc % 2, :, :]
                             ).then_inc((s_zd0, s_zd1)[hc % 2], 16)
            gp.wait_ge(s_zd0, 16 * 16)
            gp.wait_ge(s_zd1, 16 * 16)
            for p in range(4):
                gp.dma_start(zstat[32 * p:32 * p + 32, :, :],
                             zdram[:, p, :, :].rearrange("mq q c -> q mq c")
                             ).then_inc(s_zs, 16)

            wc_cast(2); wc_cast(3)
            wc_cast(4); wc_cast(5); wc_cast(6); wc_cast(7)

            def gp_mm(t, s):
                g = t * SUBS + s
                s3 = g % 2
                gp.wait_ge(s_dg, g + 1)
                gp.wait_ge(s_k, g + 1)
                for j in range(NGP):
                    h = NDV + j
                    mi = g * NGP + j
                    if mi >= 2:
                        gp.wait_ge(s_mr, mi - 1)
                    gp.tensor_tensor(
                        prodM[:, 1 + (j % 2), :].rearrange(
                            "p (a d) -> p a d", a=16),
                        eQg[:, s3, h * 64:(h + 1) * 64].unsqueeze(1)
                            .broadcast_to([128, 16, 64]),
                        eKb[:, g % 4, :].rearrange("p (a d) -> p a d", a=16),
                        op=ALU.mult).then_inc(s_gm, 1)

            def gp_out(t, s):
                g = t * SUBS + s
                s3 = g % 4
                gp.wait_ge(s_dM, g + 1)
                gp.wait_ge(s_v, g + 1)
                for h in range(ODV, 16):
                    oi = g * 16 + h
                    gp.wait_ge(s_or, oi - 1)
                    gp.tensor_tensor(
                        prodO[:, h % 2, :].rearrange("p (a e) -> p a e", a=16),
                        Mb[:, g % 2, h * 16:(h + 1) * 16].unsqueeze(2)
                            .broadcast_to([128, 16, 64]),
                        Vb[:, s3, :].rearrange("p (a e) -> p a e", a=16),
                        op=ALU.mult).then_inc(s_go, 1)

            for t in range(NT):
                gp_mm(t, 0)
                gp_out(t, 0)
                gp_mm(t, 1)
                gp_out(t, 1)
                gp_mm(t, 2)
                gp_out(t, 2)
                gp_mm(t, 3)
                gp_out(t, 3)

        @block.sync
        def _(sync):
            sync.dma_start(x_sb[:, :, :],
                           x_d[0:T, :].rearrange("(s p) d -> p s d", p=128)
                           ).then_inc(s_x, 16)
            for wsrc, wdst, need in ((wq_bf, wqt, 16), (wk_bf, wkt, 32),
                                     (wv_bf, wvt, 48)):
                sync.wait_ge(s_sg, need)
                for c in range(8):
                    sync.dma_start(wdst[:, c, :], wsrc[:, c * 128:(c + 1) * 128],
                                   transpose=True).then_inc(s_wqt, 16)

            def wct_batch(q):
                for kq in range(CPQ):
                    k = q * CPQ + kq
                    pr = k % 2
                    s_stgP = (s_stg0, s_stg1)[pr]
                    s_ssP = (s_ss0, s_ss1)[pr]
                    jcast = (k * 128 + 127) // 2048 + 1
                    sync.wait_ge(s_sg, 48 + 16 * jcast)
                    if k >= 2:
                        sync.wait_ge(s_ssP, 32 * (k // 2))
                    sync.dma_start(stage[:, pr, :],
                                   wc_bf[:, k * 128:(k + 1) * 128],
                                   transpose=True).then_inc(s_stgP, 16)
                    sync.wait_ge(s_stgP, 16 * (k // 2 + 1))
                    sync.dma_start(wct[q, 0, kq // 8, :, kq % 8, :],
                                   stage[:, pr, 0:512]).then_inc(s_ssP, 16)
                    sync.dma_start(wct[q, 1, kq // 8, :, kq % 8, :],
                                   stage[:, pr, 512:1024]).then_inc(s_ssP, 16)

            for t in range(NT):
                for q in range(NQ):
                    if q == 3 and t + 1 < NT:
                        sync.wait_ge(s_ce, 64 * t + 8)
                        sync.dma_start(
                            x_sb[:, :, :],
                            x_d[(t + 1) * T:(t + 2) * T, :].rearrange(
                                "(s p) d -> p s d", p=128)).then_inc(s_x, 16)
                    if t == 0:
                        wct_batch(q)
                    for oh in range(2):
                        for gg in range(2):
                            wl = ((t * NQ + q) * 2 + oh) * 2 + gg
                            if t == 0:
                                kmax = q * CPQ + gg * 8 + 7
                                sync.wait_ge(s_ss0, 32 * ((kmax + 1) // 2))
                                sync.wait_ge(s_ss1, 32 * ((kmax + 1) // 2))
                            if wl >= 2:
                                sync.wait_ge(s_wf, wl - 1)
                            sync.dma_start(
                                wsb[:, wl % 2, :, :],
                                wct[q, oh, gg, :, :, :]
                            ).then_inc((s_w0, s_w1)[wl % 2], 16)
                for s in range(SUBS):
                    sync.wait_ge(s_fin, 8 * t + 5 + s)
                    sync.dma_start(out_d[t * T + s * 128: t * T + (s + 1) * 128, :],
                                   partial[:, s, :]).then_inc(s_out, 16)
            sync.wait_ge(s_out, 16 * SUBS * NT)

        @block.tensor
        def _(te):
            def emit_A(t):
                for s in range(SUBS):
                    g = g_of(t, s)
                    te.wait_ge(s_ln, g + 1)
                    for half in range(2):
                        hx = 2 * g + half
                        if hx > 0:
                            te.wait_ge(s_xt, hx)
                        ins = None
                        for cc in range(4):
                            cch = half * 4 + cc
                            ins = te.transpose(pT[:, cc * 128:(cc + 1) * 128],
                                               xn_sb[:, s % 2,
                                                     cch * 128:(cch + 1) * 128],
                                               idb[:, :])
                        ins.then_inc(s_tp, 1)
                    te.wait_ge(s_xt, 2 * g + 2)
                    if t == 0 and s == 0:
                        te.wait_ge(s_wqt, 16 * 24)
                    for (proj, wt, sem_done) in ((0, wqt, s_pq),
                                                 (1, wkt, s_pk),
                                                 (2, wvt, s_pv)):
                        if proj == 0:
                            if t > 0 and s == 0:
                                te.wait_ge(s_gel, 128 * t)
                            if g > 0:
                                te.wait_ge(s_v, g)
                        elif proj == 1:
                            te.wait_ge(s_q, g + 1)
                        else:
                            te.wait_ge(s_k, g + 1)
                        ins = None
                        for oh in range(2):
                            bank = (pV0, pV1)[oh]
                            for cch in range(8):
                                ins = te.matmul(bank[:, :],
                                                xnt[:, cch, s * 128:(s + 1) * 128],
                                                wt[:, cch, oh * 512:(oh + 1) * 512],
                                                start=(cch == 0), stop=(cch == 7))
                        ins.then_inc(sem_done, 1)

            zslots = (pV0[:, :], pV1[:, :], pS[:, :])

            def z_matmul(t, qq, kq):
                zi = 128 * t + CPQ * qq + kq
                if zi >= 3:
                    te.wait_ge(s_gel, zi - 2)
                k = qq * CPQ + kq
                rem = k % 32
                c2, p2 = divmod(rem, 4)
                te.matmul(zslots[zi % 3],
                          zstat[32 * p2:32 * p2 + 32, k // 4, :],
                          xnt[32 * p2:32 * p2 + 32, c2, :],
                          start=True, stop=True,
                          tile_position=(32 * p2, 0)).then_inc(s_z, 1)

            def emit_zburst(t):
                te.wait_ge(s_zs, 64)
                te.wait_ge(s_v, SUBS * t + SUBS)
                te.wait_ge(s_xt, 8 * (t + 1))
                if t == 0:
                    te.wait_ge(s_mk, 32)
                for kq in range(CPQ):
                    z_matmul(t, 0, kq)

            def emit_C(t):
                for q in range(NQ):
                    for oh in range(2):
                        ci_g = (t * NQ + q) * 2 + oh
                        for gg in range(2):
                            wl = ci_g * 2 + gg
                            te.wait_ge((s_w0, s_w1)[wl % 2],
                                       16 * (wl // 2 + 1))
                            if oh == 0:
                                te.wait_ge(s_gel, 128 * t + CPQ * q + gg * 8 + 8)
                            for s in range(SUBS):
                                if gg == 0 and ci_g >= 1:
                                    te.wait_ge(s_ce, 4 * (ci_g - 1) + s + 1)
                                half = (pQ0, pQ1, pK0, pK1)[s][:, :]
                                for j in range(8):
                                    kq = gg * 8 + j
                                    ins = te.matmul(
                                        half,
                                        hT[:, kq, s * 128:(s + 1) * 128],
                                        wsb[:, wl % 2, j, :],
                                        start=(kq == 0), stop=(kq == CPQ - 1))
                                    if kq == CPQ - 1:
                                        ins.then_inc(s_cp, 1)
                                        if s == SUBS - 1:
                                            te.nop().then_inc(s_wf, 1)
                                    elif s == SUBS - 1 and j == 7:
                                        ins.then_inc(s_wf, 1)
                            if oh == 1 and q < NQ - 1:
                                for j in range(8):
                                    z_matmul(t, q + 1, gg * 8 + j)

            for hc in range(32):
                h, c = divmod(hc, 8)
                if hc == 0:
                    te.wait_ge(s_sl, 192)
                    te.wait_ge(s_su, SU_BASE)
                else:
                    te.wait_ge(s_eA, hc)
                ins = None
                for p in range(4):
                    pi = c * 4 + p
                    ins = te.transpose(pT[0:32, p * 32:(p + 1) * 32],
                                       bdt[:, h, pi * 32:(pi + 1) * 32],
                                       idb[0:32, 0:32])
                for p in range(4):
                    ins = te.transpose(pT[0:32, 128 + p * 64:128 + (p + 1) * 64],
                                       rupL[:, hc, p * 32:(p + 1) * 32],
                                       idb[0:64, 0:64])
                ins.then_inc(s_tA, 1)
                te.wait_ge(s_eA, hc + 1)
                if hc > 0:
                    te.wait_ge(s_mk, hc)
                ins = None
                for p in range(4):
                    ins = te.matmul(pQ0[0:32, p * 64:(p + 1) * 64],
                                    bdrm[:, p * 32:(p + 1) * 32],
                                    rupS[:, p * 64:(p + 1) * 64],
                                    start=True, stop=True)
                ins.then_inc(s_cm, 1)

            emit_A(0)
            emit_zburst(0)
            emit_C(0)
            for t in range(1, NT):
                emit_A(t)
                emit_zburst(t)
                emit_C(t)

        @block.scalar
        def _(sc):
            sc.wait_ge(s_su, 7)

            def act_square(t, s):
                g = g_of(t, s)
                sc.wait_ge(s_x, 16 * (t + 1))
                sc.activation(sqs[:, :], x_sb[:, s, :], AF.Square,
                              accum_out=sxx[:, s:s + 1]).then_inc(s_l3, 1)

            def act_rstd(t, s):
                g = g_of(t, s)
                if t > 0:
                    sc.wait_ge(s_ln, g - SUBS + 1)
                sc.wait_ge(s_l2, g + 1)
                sc.activation(lnv[:, s:s + 1], var[:, s:s + 1], AF.Ln,
                              bias=EPS).then_inc(s_ach, 1)
                ach[0] += 1
                sc.wait_ge(s_ach, ach[0])
                sc.activation(rstd[:, s:s + 1], lnv[:, s:s + 1], AF.Exp,
                              scale=-0.5).then_inc(s_l4, 1)

            for s in range(SUBS):
                act_square(0, s)
            act_rstd(0, 0)
            act_rstd(0, 1)
            for t in range(NT):
                for s in range(SUBS):
                    g = g_of(t, s)
                    s2 = g % 2
                    if t == 0 and s == 1:
                        act_rstd(0, 2)
                    if t == 0 and s == 2:
                        act_rstd(0, 3)
                    sc.wait_ge(s_pq, g + 1)
                    if g >= 2:
                        sc.wait_ge(s_dg, g - 1)
                    sc.activation(eQb[:, s2, 0:512], pV0[:, :], AF.Exp)
                    sc.activation(eQb[:, s2, 512:1024], pV1[:, :],
                                  AF.Exp).then_inc(s_q, 1)
                    sc.wait_ge(s_pk, g + 1)
                    if g >= 4:
                        sc.wait_ge(s_dM, g - 3)
                    sc.activation(eKb[:, g % 4, 0:512], pV0[:, :], AF.Exp)
                    sc.activation(eKb[:, g % 4, 512:1024], pV1[:, :],
                                  AF.Exp).then_inc(s_k, 1)
                    sc.wait_ge(s_pv, g + 1)
                    if g >= 4:
                        sc.wait_ge(s_or, 16 * (g - 3))
                    sc.activation(Vb[:, g % 4, 0:512], pV0[:, :], AF.Copy)
                    sc.activation(Vb[:, g % 4, 512:1024], pV1[:, :],
                                  AF.Copy).then_inc(s_v, 1)
                for q in range(NQ):
                    if t + 1 < NT:
                        if q == 3:
                            for s in range(SUBS):
                                act_square(t + 1, s)
                        if q == 5:
                            act_rstd(t + 1, 0); act_rstd(t + 1, 1)
                    for kq in range(CPQ):
                        zi = 128 * t + CPQ * q + kq
                        sc.wait_ge(s_z, zi + 1)
                        pg = t * NQ + q - 1
                        if pg >= 0:
                            sc.wait_ge(s_wf, pg * 4 + 2 + kq // 8 + 1)
                        sc.activation(hT[:, kq, :],
                                      (pV0[:, :], pV1[:, :],
                                       pS[:, :])[zi % 3],
                                      AF.Gelu).then_inc(s_gel, 1)
                if t + 1 < NT:
                    act_rstd(t + 1, 2); act_rstd(t + 1, 3)

        @block.vector
        def _(ve):
            def ln_stats(t, s):
                g = g_of(t, s)
                ve.wait_ge(s_x, 16 * (t + 1))
                ve.reduce_sum(sm[:, s:s + 1], x_sb[:, s, :],
                              axis=AX.X).then_inc(s_vch, 1)
                vch[0] += 1
                ve.wait_ge(s_l3, g + 1)
                ve.wait_ge(s_vch, vch[0])
                ve.tensor_scalar(out=mean[:, s:s + 1], in0=sm[:, s:s + 1],
                                 scalar1=1.0 / D, scalar2=None, op0=ALU.mult)
                ve.tensor_scalar(out=ex2[:, s:s + 1], in0=sxx[:, s:s + 1],
                                 scalar1=1.0 / D, scalar2=None,
                                 op0=ALU.mult).then_inc(s_vch, 1)
                vch[0] += 1
                ve.wait_ge(s_vch, vch[0])
                ve.tensor_mul(msq[:, s:s + 1], mean[:, s:s + 1],
                              mean[:, s:s + 1]).then_inc(s_vch, 1)
                vch[0] += 1
                ve.wait_ge(s_vch, vch[0])
                ve.tensor_sub(var[:, s:s + 1], ex2[:, s:s + 1],
                              msq[:, s:s + 1]).then_inc(s_l2, 1)
                ve.tensor_scalar(out=msq[:, s:s + 1], in0=mean[:, s:s + 1],
                                 scalar1=-1.0, scalar2=None,
                                 op0=ALU.mult).then_inc(s_vch, 1)
                vch[0] += 1

            def ln_apply(t, s):
                g = g_of(t, s)
                ve.wait_ge(s_l4, g + 1)
                ve.tensor_mul(nb[:, s:s + 1], msq[:, s:s + 1],
                              rstd[:, s:s + 1]).then_inc(s_vch, 1)
                vch[0] += 1
                ve.wait_ge(s_vch, vch[0])
                if g >= 2:
                    ve.wait_ge(s_tp, 2 * g - 2)
                ve.tensor_scalar(out=xn_sb[:, s % 2, :], in0=x_sb[:, s, :],
                                 scalar1=rstd[:, s:s + 1], scalar2=nb[:, s:s + 1],
                                 op0=ALU.mult, op1=ALU.add).then_inc(s_ln, 1)

            def ln_chain(t, s):
                ln_stats(t, s)
                ln_apply(t, s)

            def evacT(t, s):
                g = g_of(t, s)
                ve.wait_ge(s_tp, 2 * g + 1)
                ve.tensor_copy(
                    xnt[:, 0:4, s * 128:(s + 1) * 128],
                    pT[:, :].rearrange("p (c n) -> p c n", c=4)).then_inc(s_xt, 1)
                ve.wait_ge(s_tp, 2 * g + 2)
                ve.tensor_copy(
                    xnt[:, 4:8, s * 128:(s + 1) * 128],
                    pT[:, :].rearrange("p (c n) -> p c n", c=4)).then_inc(s_xt, 1)

            def prep(t, s):
                g = g_of(t, s)
                s2 = g % 2
                ve.wait_ge(s_k, g + 1)
                ve.reduce_sum(kd[:, s2, :],
                              eKb[:, g % 4, :].rearrange("p (h d) -> p d h",
                                                         d=64),
                              axis=AX.X)
                ve.wait_ge(s_q, g + 1)
                ve.reduce_sum(qd[:, s2, :],
                              eQb[:, s2, :].rearrange("p (h d) -> p h d", d=64),
                              axis=AX.X).then_inc(s_vch, 1)
                vch[0] += 1
                ve.wait_ge(s_vch, vch[0])
                ve.reciprocal(kg[:, s2, :], kd[:, s2, :])
                ve.reciprocal(rq[:, s2, :], qd[:, s2, :]).then_inc(s_vch, 1)
                vch[0] += 1
                if g >= 2:
                    ve.wait_ge(s_gm, NGP * (g - 1))
                ve.wait_ge(s_vch, vch[0])
                ve.tensor_tensor(
                    eQg[:, g % 2, :].rearrange("p (h d) -> p h d", d=64),
                    eQb[:, s2, :].rearrange("p (h d) -> p h d", d=64),
                    kg[:, s2, :].unsqueeze(1).broadcast_to([128, 16, 64]),
                    op=ALU.mult).then_inc(s_dg, 1)

            def mstep(t, s):
                g = g_of(t, s)
                s2 = g % 2
                for h in range(NDV):
                    ve.tensor_tensor(
                        prodM[:, 0, :].rearrange("p (a d) -> p a d", a=16),
                        eQg[:, g % 2, h * 64:(h + 1) * 64].unsqueeze(1)
                            .broadcast_to([128, 16, 64]),
                        eKb[:, g % 4, :].rearrange("p (a d) -> p a d", a=16),
                        op=ALU.mult)
                    ve.reduce_sum(Mf[:, h * 16:(h + 1) * 16],
                                  prodM[:, 0, :].rearrange("p (a d) -> p a d",
                                                           a=16), axis=AX.X)
                for j in range(NGP):
                    h = NDV + j
                    ve.wait_ge(s_gm, NGP * g + j + 1)
                    ve.reduce_sum(Mf[:, h * 16:(h + 1) * 16],
                                  prodM[:, 1 + (j % 2), :].rearrange(
                                      "p (a d) -> p a d", a=16),
                                  axis=AX.X).then_inc(s_mr, 1)
                if g >= 2:
                    ve.wait_ge(s_go, (16 - ODV) * (g - 1))
                ve.wait_ge(s_mr, NGP * (g + 1))
                ve.tensor_tensor(
                    Mb[:, s2, :].rearrange("p (h i) -> p h i", h=16),
                    Mf[:, :].rearrange("p (h i) -> p h i", h=16),
                    rq[:, s2, :].unsqueeze(2).broadcast_to([128, 16, 16]),
                    op=ALU.mult).then_inc(s_dM, 1)

            def outred(t, s):
                g = g_of(t, s)
                s3 = g % 4
                ve.wait_ge(s_dM, g + 1)
                ve.wait_ge(s_v, g + 1)

                def dmult(h):
                    ve.tensor_tensor(
                        prodO[:, h % 2, :].rearrange("p (a e) -> p a e", a=16),
                        Mb[:, g % 2, h * 16:(h + 1) * 16].unsqueeze(2)
                            .broadcast_to([128, 16, 64]),
                        Vb[:, s3, :].rearrange("p (a e) -> p a e", a=16),
                        op=ALU.mult)

                def dred(h):
                    if h >= ODV:
                        ve.wait_ge(s_go, (16 - ODV) * g + (h - ODV) + 1)
                    with nc.allow_low_precision(
                            reason="DVE reduce accumulates fp32 internally; "
                                   "bf16 only on the final write"):
                        ve.reduce_sum(attn[:, s, h * 64:(h + 1) * 64],
                                      prodO[:, h % 2, :].rearrange(
                                          "p (a e) -> p e a", a=16),
                                      axis=AX.X).then_inc(s_or, 1)

                dmult(0)
                dmult(1)
                for h in range(ODV - 2):
                    dred(h)
                    dmult(h + 2)
                dred(ODV - 2)
                dred(ODV - 1)
                for h in range(ODV, 16):
                    dred(h)

            def cevac(t, ci):
                q, oh = divmod(ci, 2)
                ci_g = (t * NQ + q) * 2 + oh
                for s in range(SUBS):
                    ve.wait_ge(s_cp, (t * NQ + q) * 8 + oh * 4 + s + 1)
                    half = (pQ0, pQ1, pK0, pK1)[s][:, :]
                    dst = partial[:, s, oh * 512:(oh + 1) * 512]
                    if q == 0:
                        if t > 0:
                            ve.wait_ge(s_out, 64 * t)
                        ve.tensor_add(dst, half,
                                      x_sb[:, s, oh * 512:(oh + 1) * 512]
                                      ).then_inc(s_ce, 1)
                    elif q < NQ - 1:
                        ve.tensor_add(dst, half, dst).then_inc(s_ce, 1)
                    else:
                        ve.tensor_add(dst, half, dst).then_inc(s_ce, 1)
                        ve.tensor_add(dst, dst,
                                      attn[:, s, oh * 512:(oh + 1) * 512]
                                      ).then_inc(s_fin, 1)

            def main_tile(t):
                mstep(t, 0)
                outred(t, 0)
                mstep(t, 1)
                outred(t, 1)
                cevac(t, 0); cevac(t, 1)
                prep(t, 2); mstep(t, 2)
                outred(t, 2)
                cevac(t, 2); cevac(t, 3); cevac(t, 4); cevac(t, 5)
                if t + 1 < NT:
                    ln_stats(t + 1, 0); ln_stats(t + 1, 1)
                prep(t, 3); mstep(t, 3)
                outred(t, 3)
                cevac(t, 6); cevac(t, 7); cevac(t, 8); cevac(t, 9)
                cevac(t, 10); cevac(t, 11); cevac(t, 12)
                cevac(t, 13); cevac(t, 14)
                if t + 1 < NT:
                    ln_stats(t + 1, 2); ln_stats(t + 1, 3)
                    ln_apply(t + 1, 0); ln_apply(t + 1, 1)
                cevac(t, 15)

            for hc in range(32):
                ve.wait_ge(s_tA, hc + 1)
                ve.tensor_copy(bdrm[:, :], pT[0:32, 0:128])
                ve.tensor_copy(rupS[:, :], pT[0:32, 128:384]).then_inc(s_eA, 1)
                ve.wait_ge(s_cm, hc + 1)
                if hc >= 2:
                    ve.wait_ge((s_zd0, s_zd1)[hc % 2], 16 * (hc // 2))
                ins = None
                for p in range(4):
                    ve.tensor_scalar(
                        out=zstg[:, hc % 2, p, 0:64],
                        in0=pQ0[0:32, p * 64:(p + 1) * 64],
                        scalar1=msk[:, 0:1], scalar2=None, op0=ALU.mult)
                    ins = ve.tensor_scalar(
                        out=zstg[:, hc % 2, p, 64:128],
                        in0=pQ0[0:32, p * 64:(p + 1) * 64],
                        scalar1=msk[:, 1:2], scalar2=None, op0=ALU.mult)
                ins.then_inc(s_mk, 1)

            ln_chain(0, 0); ln_chain(0, 1); evacT(0, 0)
            ln_chain(0, 2); evacT(0, 1)
            prep(0, 0)
            ln_chain(0, 3); evacT(0, 2)
            prep(0, 1)
            evacT(0, 3)
            main_tile(0)
            for t in range(1, NT):
                evacT(t, 0); evacT(t, 1)
                ln_apply(t, 2); ln_apply(t, 3)
                prep(t, 0)
                evacT(t, 2)
                prep(t, 1)
                evacT(t, 3)
                main_tile(t)

    return nc


def kernel(L0_attn_input_seq_embs, L0_attn_W_Q, L0_attn_W_K, L0_attn_W_V,
           L0_ffn_rail_maker, L0_ffn_rail_up_proj, L0_ffn_w_cproj):
    x = np.ascontiguousarray(np.asarray(L0_attn_input_seq_embs, dtype=np.float32))
    wq = np.ascontiguousarray(np.asarray(L0_attn_W_Q, dtype=np.float32))
    wk = np.ascontiguousarray(np.asarray(L0_attn_W_K, dtype=np.float32))
    wv = np.ascontiguousarray(np.asarray(L0_attn_W_V, dtype=np.float32))
    rm = np.ascontiguousarray(np.asarray(L0_ffn_rail_maker, dtype=np.float32))
    rup = np.ascontiguousarray(np.asarray(L0_ffn_rail_up_proj, dtype=np.float32))
    wc = np.ascontiguousarray(np.asarray(L0_ffn_w_cproj, dtype=np.float32))

    nc = build_core_program(L=x.shape[1])
    in_maps = [{"x": np.ascontiguousarray(x[b]), "wq": wq, "wk": wk, "wv": wv,
                "rm": rm, "rup": rup, "wc": wc} for b in range(x.shape[0])]
    res = run_bass_kernel_spmd(nc, in_maps, core_ids=list(range(x.shape[0])))
    return np.stack([np.asarray(r["out"], dtype=np.float32) for r in res.results],
                    axis=0)


# version 06
# speedup vs baseline: 1.2632x; 1.2632x over previous
import numpy as np
from contextlib import ExitStack
import concourse.bass as bass
import concourse.mybir as mybir
from concourse.bass_utils import run_bass_kernel_spmd

F32 = mybir.dt.float32
BF16 = mybir.dt.bfloat16
AF = mybir.ActivationFunctionType
ALU = mybir.AluOpType
AX = mybir.AxisListType

D = 1024
T = 512
SUBS = 4
NQ = 8
CPQ = 16
EPS = 1e-5
NGP = 10
ODV = 4
NDV = 16 - NGP
SU_BASE = 8


def build_core_program(L=8192):
    NT = L // T
    nc = bass.Bass("TRN2", target_bir_lowering=False)

    x_d = nc.dram_tensor("x", [L, D], F32, kind="ExternalInput").ap()
    wq_d = nc.dram_tensor("wq", [D, D], F32, kind="ExternalInput").ap()
    wk_d = nc.dram_tensor("wk", [D, D], F32, kind="ExternalInput").ap()
    wv_d = nc.dram_tensor("wv", [D, D], F32, kind="ExternalInput").ap()
    rm_d = nc.dram_tensor("rm", [4, 64, 16, 16], F32, kind="ExternalInput").ap()
    rup_d = nc.dram_tensor("rup", [4, 64, 64, 16], F32, kind="ExternalInput").ap()
    wc_d = nc.dram_tensor("wc", [D, 16384], F32, kind="ExternalInput").ap()
    out_d = nc.dram_tensor("out", [L, D], F32, kind="ExternalOutput").ap()

    wq_bf = nc.dram_tensor("wq_bf", [D, D], BF16).ap()
    wk_bf = nc.dram_tensor("wk_bf", [D, D], BF16).ap()
    wv_bf = nc.dram_tensor("wv_bf", [D, D], BF16).ap()
    wc_bf = nc.dram_tensor("wc_bf", [D, 16384], BF16).ap()
    wct = nc.dram_tensor("wct", [NQ, 2, 2, 128, 8, 512], BF16).ap()
    zdram = nc.dram_tensor("zdram", [32, 4, 32, 128], BF16).ap()

    with ExitStack() as ES:
        wqt = ES.enter_context(nc.sbuf_tensor("wqt", [128, 8, D], BF16))
        wkt = ES.enter_context(nc.sbuf_tensor("wkt", [128, 8, D], BF16))
        wvt = ES.enter_context(nc.sbuf_tensor("wvt", [128, 8, D], BF16))
        zstat = ES.enter_context(nc.sbuf_tensor("zstat", [128, 32, 128], BF16))
        idb = ES.enter_context(nc.sbuf_tensor("idb", [128, 128], BF16))
        msk = ES.enter_context(nc.sbuf_tensor("msk", [32, 2], F32))
        x_sb = ES.enter_context(nc.sbuf_tensor("x_sb", [128, SUBS, D], F32))
        xn_sb = ES.enter_context(nc.sbuf_tensor("xn_sb", [128, 2, D], BF16))
        xnt = ES.enter_context(nc.sbuf_tensor("xnt", [128, 8, T], BF16))
        eQb = ES.enter_context(nc.sbuf_tensor("eQb", [128, 2, D], BF16))
        eKb = ES.enter_context(nc.sbuf_tensor("eKb", [128, 4, D], BF16))
        eQg = ES.enter_context(nc.sbuf_tensor("eQg", [128, 2, D], BF16))
        Vb = ES.enter_context(nc.sbuf_tensor("Vb", [128, 4, D], BF16))
        kd = ES.enter_context(nc.sbuf_tensor("kd", [128, 2, 64], F32))
        kg = ES.enter_context(nc.sbuf_tensor("kg", [128, 2, 64], F32))
        qd = ES.enter_context(nc.sbuf_tensor("qd", [128, 2, 16], F32))
        rq = ES.enter_context(nc.sbuf_tensor("rq", [128, 2, 16], F32))
        Mf = ES.enter_context(nc.sbuf_tensor("Mf", [128, 256], F32))
        Mb = ES.enter_context(nc.sbuf_tensor("Mb", [128, 2, 256], BF16))
        prodM = ES.enter_context(nc.sbuf_tensor("prodM", [128, 3, D], BF16))
        prodO = ES.enter_context(nc.sbuf_tensor("prodO", [128, 2, D], F32))
        attn = ES.enter_context(nc.sbuf_tensor("attn", [128, SUBS, D], BF16))
        hT = ES.enter_context(nc.sbuf_tensor("hT", [128, CPQ, T], BF16))
        wsb = ES.enter_context(nc.sbuf_tensor("wsb", [128, 2, 8, 512], BF16))
        partial = ES.enter_context(nc.sbuf_tensor("partial", [128, SUBS, D], F32))
        sm = ES.enter_context(nc.sbuf_tensor("sm", [128, SUBS], F32))
        sxx = ES.enter_context(nc.sbuf_tensor("sxx", [128, SUBS], F32))
        mean = ES.enter_context(nc.sbuf_tensor("mean", [128, SUBS], F32))
        ex2 = ES.enter_context(nc.sbuf_tensor("ex2", [128, SUBS], F32))
        msq = ES.enter_context(nc.sbuf_tensor("msq", [128, SUBS], F32))
        var = ES.enter_context(nc.sbuf_tensor("var", [128, SUBS], F32))
        lnv = ES.enter_context(nc.sbuf_tensor("lnv", [128, SUBS], F32))
        rstd = ES.enter_context(nc.sbuf_tensor("rstd", [128, SUBS], F32))
        nb = ES.enter_context(nc.sbuf_tensor("nb", [128, SUBS], F32))
        stage = ES.enter_context(nc.sbuf_tensor("stage", [128, 2, D], BF16))
        bdt = ES.enter_context(nc.sbuf_tensor("bdt", [32, 4, D], BF16))
        bdrm = ES.enter_context(nc.sbuf_tensor("bdrm", [32, 128], BF16))
        rupL = ES.enter_context(nc.sbuf_tensor("rupL", [64, 32, 128], BF16))
        rupS = ES.enter_context(nc.sbuf_tensor("rupS", [32, 256], BF16))
        zstg = ES.enter_context(nc.sbuf_tensor("zstg", [32, 2, 4, 128], BF16))
        cst = ES.enter_context(nc.sbuf_tensor("cst", [128, 2], F32))
        sqs = ES.enter_context(nc.sbuf_tensor("sqs", [128, D], BF16))
        pQ0 = ES.enter_context(nc.psum_tensor("pQ0", [128, 512], F32))
        pQ1 = ES.enter_context(nc.psum_tensor("pQ1", [128, 512], F32))
        pK0 = ES.enter_context(nc.psum_tensor("pK0", [128, 512], F32))
        pK1 = ES.enter_context(nc.psum_tensor("pK1", [128, 512], F32))
        pV0 = ES.enter_context(nc.psum_tensor("pV0", [128, 512], F32))
        pV1 = ES.enter_context(nc.psum_tensor("pV1", [128, 512], F32))
        pT = ES.enter_context(nc.psum_tensor("pT", [128, 512], BF16))
        pS = ES.enter_context(nc.psum_tensor("pS", [128, 512], F32))
        s_sg = ES.enter_context(nc.semaphore("s_sg"))
        s_su = ES.enter_context(nc.semaphore("s_su"))
        s_sl = ES.enter_context(nc.semaphore("s_sl"))
        s_tA = ES.enter_context(nc.semaphore("s_tA"))
        s_eA = ES.enter_context(nc.semaphore("s_eA"))
        s_cm = ES.enter_context(nc.semaphore("s_cm"))
        s_mk = ES.enter_context(nc.semaphore("s_mk"))
        s_zd0 = ES.enter_context(nc.semaphore("s_zd0"))
        s_zd1 = ES.enter_context(nc.semaphore("s_zd1"))
        s_zs = ES.enter_context(nc.semaphore("s_zs"))
        s_stg0 = ES.enter_context(nc.semaphore("s_stg0"))
        s_stg1 = ES.enter_context(nc.semaphore("s_stg1"))
        s_ss0 = ES.enter_context(nc.semaphore("s_ss0"))
        s_ss1 = ES.enter_context(nc.semaphore("s_ss1"))
        s_wqt = ES.enter_context(nc.semaphore("s_wqt"))
        s_x = ES.enter_context(nc.semaphore("s_x"))
        s_w0 = ES.enter_context(nc.semaphore("s_w0"))
        s_w1 = ES.enter_context(nc.semaphore("s_w1"))
        s_wf = ES.enter_context(nc.semaphore("s_wf"))
        s_ln = ES.enter_context(nc.semaphore("s_ln"))
        s_l2 = ES.enter_context(nc.semaphore("s_l2"))
        s_l3 = ES.enter_context(nc.semaphore("s_l3"))
        s_l4 = ES.enter_context(nc.semaphore("s_l4"))
        s_tp = ES.enter_context(nc.semaphore("s_tp"))
        s_xt = ES.enter_context(nc.semaphore("s_xt"))
        s_pq = ES.enter_context(nc.semaphore("s_pq"))
        s_pk = ES.enter_context(nc.semaphore("s_pk"))
        s_pv = ES.enter_context(nc.semaphore("s_pv"))
        s_q = ES.enter_context(nc.semaphore("s_q"))
        s_k = ES.enter_context(nc.semaphore("s_k"))
        s_v = ES.enter_context(nc.semaphore("s_v"))
        s_dg = ES.enter_context(nc.semaphore("s_dg"))
        s_gm = ES.enter_context(nc.semaphore("s_gm"))
        s_mr = ES.enter_context(nc.semaphore("s_mr"))
        s_dM = ES.enter_context(nc.semaphore("s_dM"))
        s_go = ES.enter_context(nc.semaphore("s_go"))
        s_or = ES.enter_context(nc.semaphore("s_or"))
        s_z = ES.enter_context(nc.semaphore("s_z"))
        s_gel = ES.enter_context(nc.semaphore("s_gel"))
        s_cp = ES.enter_context(nc.semaphore("s_cp"))
        s_ce = ES.enter_context(nc.semaphore("s_ce"))
        s_fin = ES.enter_context(nc.semaphore("s_fin"))
        s_vch = ES.enter_context(nc.semaphore("s_vch"))
        s_ach = ES.enter_context(nc.semaphore("s_ach"))
        s_out = ES.enter_context(nc.semaphore("s_out"))
        block = ES.enter_context(nc.Block())

        nc.const_aps.register(F32, EPS, cst[:, 0:1])
        nc.const_aps.register(F32, -0.5, cst[:, 1:2])

        vch = [0]
        ach = [0]

        def g_of(t, s):
            return t * SUBS + s

        @block.gpsimd
        def _(gp):
            gp.memset(idb[:, :], 0.0).then_inc(s_su, 1)
            gp.wait_ge(s_su, 1)
            gp.affine_select(out=idb[:, :], in_=idb[:, :], compare_op=ALU.not_equal,
                             fill=1.0, base=0, pattern=[[-1, 128]],
                             channel_multiplier=1).then_inc(s_su, 1)
            gp.memset(msk[:, :], 1.0).then_inc(s_su, 1)
            gp.wait_ge(s_su, 3)
            gp.affine_select(out=msk[:, 0:1], in_=msk[:, 0:1], compare_op=ALU.is_ge,
                             fill=0.0, base=15, pattern=[[0, 1]],
                             channel_multiplier=-1).then_inc(s_su, 1)
            gp.affine_select(out=msk[:, 1:2], in_=msk[:, 1:2], compare_op=ALU.is_ge,
                             fill=0.0, base=-16, pattern=[[0, 1]],
                             channel_multiplier=1).then_inc(s_su, 1)
            gp.memset(cst[:, 0:1], EPS).then_inc(s_su, 1)
            gp.memset(cst[:, 1:2], -0.5).then_inc(s_su, 1)
            gp.memset(bdt[:, :, :], 0.0).then_inc(s_su, 1)

            for h in range(4):
                for r2 in range(2):
                    src = rm_d[h, :, :, :].rearrange(
                        "(c p r) i j -> r i (c p) j", c=8, p=4, r=2)[r2, :, :, :]
                    dst = bdt[16 * r2:16 * r2 + 16, h, :].rearrange(
                        "i (pi q) -> i pi q", q=32)[:, :, 16 * r2:16 * r2 + 16]
                    gp.dma_start(dst, src).then_inc(s_sl, 16)
            for h in range(4):
                src = rup_d[h, :, :, :].rearrange("(c p r) k j -> k c p r j",
                                                  c=8, p=4, r=2)
                dst = rupL[:, 8 * h:8 * h + 8, :].rearrange(
                    "k c (p r j) -> k c p r j", p=4, r=2)
                gp.dma_start(dst, src).then_inc(s_sl, 16)

            sgc = [0]

            def cast_dma(dst, srcp):
                if sgc[0] > 0:
                    gp.wait_ge(s_sg, sgc[0])
                gp.dma_start(dst, srcp).then_inc(s_sg, 16)
                sgc[0] += 16

            cast_dma(wq_bf[:, :], wq_d[:, :])
            cast_dma(wk_bf[:, :], wk_d[:, :])
            cast_dma(wv_bf[:, :], wv_d[:, :])

            def wc_cast(j):
                cast_dma(wc_bf[:, j * 2048:(j + 1) * 2048],
                         wc_d[:, j * 2048:(j + 1) * 2048])
            wc_cast(0); wc_cast(1)

            for hc in range(32):
                gp.wait_ge(s_mk, hc + 1)
                gp.dma_start(zdram[hc, :, :, :].rearrange("p q c -> q p c"),
                             zstg[:, hc % 2, :, :]
                             ).then_inc((s_zd0, s_zd1)[hc % 2], 16)
            gp.wait_ge(s_zd0, 16 * 16)
            gp.wait_ge(s_zd1, 16 * 16)
            for p in range(4):
                gp.dma_start(zstat[32 * p:32 * p + 32, :, :],
                             zdram[:, p, :, :].rearrange("mq q c -> q mq c")
                             ).then_inc(s_zs, 16)

            wc_cast(2); wc_cast(3)
            wc_cast(4); wc_cast(5); wc_cast(6); wc_cast(7)

            def gp_mm(t, s):
                g = t * SUBS + s
                s3 = g % 2
                gp.wait_ge(s_dg, g + 1)
                gp.wait_ge(s_k, g + 1)
                for j in range(NGP):
                    h = NDV + j
                    mi = g * NGP + j
                    if mi >= 2:
                        gp.wait_ge(s_mr, mi - 1)
                    gp.tensor_tensor(
                        prodM[:, 1 + (j % 2), :].rearrange(
                            "p (a d) -> p a d", a=16),
                        eQg[:, s3, h * 64:(h + 1) * 64].unsqueeze(1)
                            .broadcast_to([128, 16, 64]),
                        eKb[:, g % 4, :].rearrange("p (a d) -> p a d", a=16),
                        op=ALU.mult).then_inc(s_gm, 1)

            def gp_out(t, s):
                g = t * SUBS + s
                s3 = g % 4
                gp.wait_ge(s_dM, g + 1)
                gp.wait_ge(s_v, g + 1)
                for h in range(ODV, 16):
                    oi = g * 16 + h
                    gp.wait_ge(s_or, oi - 1)
                    gp.tensor_tensor(
                        prodO[:, h % 2, :].rearrange("p (a e) -> p a e", a=16),
                        Mb[:, g % 2, h * 16:(h + 1) * 16].unsqueeze(2)
                            .broadcast_to([128, 16, 64]),
                        Vb[:, s3, :].rearrange("p (a e) -> p a e", a=16),
                        op=ALU.mult).then_inc(s_go, 1)

            for t in range(NT):
                gp_mm(t, 0)
                gp_out(t, 0)
                gp_mm(t, 1)
                gp_out(t, 1)
                gp_mm(t, 2)
                gp_out(t, 2)
                gp_mm(t, 3)
                gp_out(t, 3)

        @block.sync
        def _(sync):
            sync.dma_start(x_sb[:, :, :],
                           x_d[0:T, :].rearrange("(s p) d -> p s d", p=128)
                           ).then_inc(s_x, 16)
            for wsrc, wdst, need in ((wq_bf, wqt, 16), (wk_bf, wkt, 32),
                                     (wv_bf, wvt, 48)):
                sync.wait_ge(s_sg, need)
                for c in range(8):
                    sync.dma_start(wdst[:, c, :], wsrc[:, c * 128:(c + 1) * 128],
                                   transpose=True).then_inc(s_wqt, 16)

            def wct_batch(q):
                for kq in range(CPQ):
                    k = q * CPQ + kq
                    pr = k % 2
                    s_stgP = (s_stg0, s_stg1)[pr]
                    s_ssP = (s_ss0, s_ss1)[pr]
                    jcast = (k * 128 + 127) // 2048 + 1
                    sync.wait_ge(s_sg, 48 + 16 * jcast)
                    if k >= 2:
                        sync.wait_ge(s_ssP, 32 * (k // 2))
                    sync.dma_start(stage[:, pr, :],
                                   wc_bf[:, k * 128:(k + 1) * 128],
                                   transpose=True).then_inc(s_stgP, 16)
                    sync.wait_ge(s_stgP, 16 * (k // 2 + 1))
                    sync.dma_start(wct[q, 0, kq // 8, :, kq % 8, :],
                                   stage[:, pr, 0:512]).then_inc(s_ssP, 16)
                    sync.dma_start(wct[q, 1, kq // 8, :, kq % 8, :],
                                   stage[:, pr, 512:1024]).then_inc(s_ssP, 16)

            for t in range(NT):
                for q in range(NQ):
                    if q == 3 and t + 1 < NT:
                        sync.wait_ge(s_ce, 64 * t + 8)
                        sync.dma_start(
                            x_sb[:, :, :],
                            x_d[(t + 1) * T:(t + 2) * T, :].rearrange(
                                "(s p) d -> p s d", p=128)).then_inc(s_x, 16)
                    if t == 0:
                        wct_batch(q)
                    for oh in range(2):
                        for gg in range(2):
                            wl = ((t * NQ + q) * 2 + oh) * 2 + gg
                            if t == 0:
                                kmax = q * CPQ + gg * 8 + 7
                                sync.wait_ge(s_ss0, 32 * ((kmax + 1) // 2))
                                sync.wait_ge(s_ss1, 32 * ((kmax + 1) // 2))
                            if wl >= 2:
                                sync.wait_ge(s_wf, wl - 1)
                            sync.dma_start(
                                wsb[:, wl % 2, :, :],
                                wct[q, oh, gg, :, :, :]
                            ).then_inc((s_w0, s_w1)[wl % 2], 16)
                for s in range(SUBS):
                    sync.wait_ge(s_fin, 8 * t + 5 + s)
                    sync.dma_start(out_d[t * T + s * 128: t * T + (s + 1) * 128, :],
                                   partial[:, s, :]).then_inc(s_out, 16)
            sync.wait_ge(s_out, 16 * SUBS * NT)

        @block.tensor
        def _(te):
            def emit_A(t):
                for s in range(SUBS):
                    g = g_of(t, s)
                    te.wait_ge(s_ln, g + 1)
                    for half in range(2):
                        hx = 2 * g + half
                        if hx > 0:
                            te.wait_ge(s_xt, hx)
                        ins = None
                        for cc in range(4):
                            cch = half * 4 + cc
                            ins = te.transpose(pT[:, cc * 128:(cc + 1) * 128],
                                               xn_sb[:, s % 2,
                                                     cch * 128:(cch + 1) * 128],
                                               idb[:, :])
                        ins.then_inc(s_tp, 1)
                    te.wait_ge(s_xt, 2 * g + 2)
                    if t == 0 and s == 0:
                        te.wait_ge(s_wqt, 16 * 24)
                    for (proj, wt, sem_done) in ((0, wqt, s_pq),
                                                 (1, wkt, s_pk),
                                                 (2, wvt, s_pv)):
                        if proj == 0:
                            if t > 0 and s == 0:
                                te.wait_ge(s_gel, 128 * t)
                            if g > 0:
                                te.wait_ge(s_v, g)
                        elif proj == 1:
                            te.wait_ge(s_q, g + 1)
                        else:
                            te.wait_ge(s_k, g + 1)
                        ins = None
                        for oh in range(2):
                            bank = (pV0, pV1)[oh]
                            for cch in range(8):
                                ins = te.matmul(bank[:, :],
                                                xnt[:, cch, s * 128:(s + 1) * 128],
                                                wt[:, cch, oh * 512:(oh + 1) * 512],
                                                start=(cch == 0), stop=(cch == 7))
                        ins.then_inc(sem_done, 1)

            zslots = (pV0[:, :], pV1[:, :], pS[:, :])

            def z_matmul(t, qq, kq):
                zi = 128 * t + CPQ * qq + kq
                if zi >= 3:
                    te.wait_ge(s_gel, zi - 2)
                k = qq * CPQ + kq
                rem = k % 32
                c2, p2 = divmod(rem, 4)
                te.matmul(zslots[zi % 3],
                          zstat[32 * p2:32 * p2 + 32, k // 4, :],
                          xnt[32 * p2:32 * p2 + 32, c2, :],
                          start=True, stop=True,
                          tile_position=(32 * p2, 0)).then_inc(s_z, 1)

            def emit_zburst(t):
                te.wait_ge(s_zs, 64)
                te.wait_ge(s_v, SUBS * t + SUBS)
                te.wait_ge(s_xt, 8 * (t + 1))
                if t == 0:
                    te.wait_ge(s_mk, 32)
                for kq in range(CPQ):
                    z_matmul(t, 0, kq)

            def emit_C(t):
                for q in range(NQ):
                    for oh in range(2):
                        ci_g = (t * NQ + q) * 2 + oh
                        for gg in range(2):
                            wl = ci_g * 2 + gg
                            te.wait_ge((s_w0, s_w1)[wl % 2],
                                       16 * (wl // 2 + 1))
                            if oh == 0:
                                te.wait_ge(s_gel, 128 * t + CPQ * q + gg * 8 + 8)
                            for s in range(SUBS):
                                if gg == 0 and ci_g >= 1:
                                    te.wait_ge(s_ce, 4 * (ci_g - 1) + s + 1)
                                half = (pQ0, pQ1, pK0, pK1)[s][:, :]
                                for j in range(8):
                                    kq = gg * 8 + j
                                    ins = te.matmul(
                                        half,
                                        hT[:, kq, s * 128:(s + 1) * 128],
                                        wsb[:, wl % 2, j, :],
                                        start=(kq == 0), stop=(kq == CPQ - 1))
                                    if kq == CPQ - 1:
                                        ins.then_inc(s_cp, 1)
                                        if s == SUBS - 1:
                                            te.nop().then_inc(s_wf, 1)
                                    elif s == SUBS - 1 and j == 7:
                                        ins.then_inc(s_wf, 1)
                            if oh == 1 and q < NQ - 1:
                                for j in range(8):
                                    z_matmul(t, q + 1, gg * 8 + j)

            for hc in range(32):
                h, c = divmod(hc, 8)
                if hc == 0:
                    te.wait_ge(s_sl, 192)
                    te.wait_ge(s_su, SU_BASE)
                else:
                    te.wait_ge(s_eA, hc)
                ins = None
                for p in range(4):
                    pi = c * 4 + p
                    ins = te.transpose(pT[0:32, p * 32:(p + 1) * 32],
                                       bdt[:, h, pi * 32:(pi + 1) * 32],
                                       idb[0:32, 0:32])
                for p in range(4):
                    ins = te.transpose(pT[0:32, 128 + p * 64:128 + (p + 1) * 64],
                                       rupL[:, hc, p * 32:(p + 1) * 32],
                                       idb[0:64, 0:64])
                ins.then_inc(s_tA, 1)
                te.wait_ge(s_eA, hc + 1)
                if hc > 0:
                    te.wait_ge(s_mk, hc)
                ins = None
                for p in range(4):
                    ins = te.matmul(pQ0[0:32, p * 64:(p + 1) * 64],
                                    bdrm[:, p * 32:(p + 1) * 32],
                                    rupS[:, p * 64:(p + 1) * 64],
                                    start=True, stop=True)
                ins.then_inc(s_cm, 1)

            emit_A(0)
            emit_zburst(0)
            emit_C(0)
            for t in range(1, NT):
                emit_A(t)
                emit_zburst(t)
                emit_C(t)

        @block.scalar
        def _(sc):
            sc.wait_ge(s_su, 7)

            def act_square(t, s):
                g = g_of(t, s)
                sc.wait_ge(s_x, 16 * (t + 1))
                sc.activation(sqs[:, :], x_sb[:, s, :], AF.Square,
                              accum_out=sxx[:, s:s + 1]).then_inc(s_l3, 1)

            def act_rstd(t, s):
                g = g_of(t, s)
                if t > 0:
                    sc.wait_ge(s_ln, g - SUBS + 1)
                sc.wait_ge(s_l2, g + 1)
                sc.activation(lnv[:, s:s + 1], var[:, s:s + 1], AF.Ln,
                              bias=EPS).then_inc(s_ach, 1)
                ach[0] += 1
                sc.wait_ge(s_ach, ach[0])
                sc.activation(rstd[:, s:s + 1], lnv[:, s:s + 1], AF.Exp,
                              scale=-0.5).then_inc(s_l4, 1)

            for s in range(SUBS):
                act_square(0, s)
            act_rstd(0, 0)
            act_rstd(0, 1)
            for t in range(NT):
                for s in range(SUBS):
                    g = g_of(t, s)
                    s2 = g % 2
                    if t == 0 and s == 1:
                        act_rstd(0, 2)
                    if t == 0 and s == 2:
                        act_rstd(0, 3)
                    sc.wait_ge(s_pq, g + 1)
                    if g >= 2:
                        sc.wait_ge(s_dg, g - 1)
                    sc.activation(eQb[:, s2, 0:512], pV0[:, :], AF.Exp)
                    sc.activation(eQb[:, s2, 512:1024], pV1[:, :],
                                  AF.Exp).then_inc(s_q, 1)
                    sc.wait_ge(s_pk, g + 1)
                    if g >= 4:
                        sc.wait_ge(s_dM, g - 3)
                    sc.activation(eKb[:, g % 4, 0:512], pV0[:, :], AF.Exp)
                    sc.activation(eKb[:, g % 4, 512:1024], pV1[:, :],
                                  AF.Exp).then_inc(s_k, 1)
                    sc.wait_ge(s_pv, g + 1)
                    if g >= 4:
                        sc.wait_ge(s_or, 16 * (g - 3))
                    sc.activation(Vb[:, g % 4, 0:512], pV0[:, :], AF.Copy)
                    sc.activation(Vb[:, g % 4, 512:1024], pV1[:, :],
                                  AF.Copy).then_inc(s_v, 1)
                for q in range(NQ):
                    if t + 1 < NT:
                        if q == 3:
                            for s in range(SUBS):
                                act_square(t + 1, s)
                        if q == 7:
                            act_rstd(t + 1, 0); act_rstd(t + 1, 1)
                    for kq in range(CPQ):
                        zi = 128 * t + CPQ * q + kq
                        sc.wait_ge(s_z, zi + 1)
                        pg = t * NQ + q - 1
                        if pg >= 0:
                            sc.wait_ge(s_wf, pg * 4 + 2 + kq // 8 + 1)
                        sc.activation(hT[:, kq, :],
                                      (pV0[:, :], pV1[:, :],
                                       pS[:, :])[zi % 3],
                                      AF.Gelu).then_inc(s_gel, 1)
                if t + 1 < NT:
                    act_rstd(t + 1, 2); act_rstd(t + 1, 3)

        @block.vector
        def _(ve):
            def ln_stats(t, s):
                g = g_of(t, s)
                ve.wait_ge(s_x, 16 * (t + 1))
                ve.reduce_sum(sm[:, s:s + 1], x_sb[:, s, :],
                              axis=AX.X).then_inc(s_vch, 1)
                vch[0] += 1
                ve.wait_ge(s_l3, g + 1)
                ve.wait_ge(s_vch, vch[0])
                ve.tensor_scalar(out=mean[:, s:s + 1], in0=sm[:, s:s + 1],
                                 scalar1=1.0 / D, scalar2=None, op0=ALU.mult)
                ve.tensor_scalar(out=ex2[:, s:s + 1], in0=sxx[:, s:s + 1],
                                 scalar1=1.0 / D, scalar2=None,
                                 op0=ALU.mult).then_inc(s_vch, 1)
                vch[0] += 1
                ve.wait_ge(s_vch, vch[0])
                ve.tensor_mul(msq[:, s:s + 1], mean[:, s:s + 1],
                              mean[:, s:s + 1]).then_inc(s_vch, 1)
                vch[0] += 1
                ve.wait_ge(s_vch, vch[0])
                ve.tensor_sub(var[:, s:s + 1], ex2[:, s:s + 1],
                              msq[:, s:s + 1]).then_inc(s_l2, 1)
                ve.tensor_scalar(out=msq[:, s:s + 1], in0=mean[:, s:s + 1],
                                 scalar1=-1.0, scalar2=None,
                                 op0=ALU.mult).then_inc(s_vch, 1)
                vch[0] += 1

            def ln_apply(t, s):
                g = g_of(t, s)
                ve.wait_ge(s_l4, g + 1)
                ve.tensor_mul(nb[:, s:s + 1], msq[:, s:s + 1],
                              rstd[:, s:s + 1]).then_inc(s_vch, 1)
                vch[0] += 1
                ve.wait_ge(s_vch, vch[0])
                if g >= 2:
                    ve.wait_ge(s_tp, 2 * g - 2)
                ve.tensor_scalar(out=xn_sb[:, s % 2, :], in0=x_sb[:, s, :],
                                 scalar1=rstd[:, s:s + 1], scalar2=nb[:, s:s + 1],
                                 op0=ALU.mult, op1=ALU.add).then_inc(s_ln, 1)

            def ln_chain(t, s):
                ln_stats(t, s)
                ln_apply(t, s)

            def evacT(t, s):
                g = g_of(t, s)
                ve.wait_ge(s_tp, 2 * g + 1)
                ve.tensor_copy(
                    xnt[:, 0:4, s * 128:(s + 1) * 128],
                    pT[:, :].rearrange("p (c n) -> p c n", c=4)).then_inc(s_xt, 1)
                ve.wait_ge(s_tp, 2 * g + 2)
                ve.tensor_copy(
                    xnt[:, 4:8, s * 128:(s + 1) * 128],
                    pT[:, :].rearrange("p (c n) -> p c n", c=4)).then_inc(s_xt, 1)

            def prep(t, s):
                g = g_of(t, s)
                s2 = g % 2
                ve.wait_ge(s_k, g + 1)
                ve.reduce_sum(kd[:, s2, :],
                              eKb[:, g % 4, :].rearrange("p (h d) -> p d h",
                                                         d=64),
                              axis=AX.X)
                ve.wait_ge(s_q, g + 1)
                ve.reduce_sum(qd[:, s2, :],
                              eQb[:, s2, :].rearrange("p (h d) -> p h d", d=64),
                              axis=AX.X).then_inc(s_vch, 1)
                vch[0] += 1
                ve.wait_ge(s_vch, vch[0])
                ve.reciprocal(kg[:, s2, :], kd[:, s2, :])
                ve.reciprocal(rq[:, s2, :], qd[:, s2, :]).then_inc(s_vch, 1)
                vch[0] += 1
                if g >= 2:
                    ve.wait_ge(s_gm, NGP * (g - 1))
                ve.wait_ge(s_vch, vch[0])
                ve.tensor_tensor(
                    eQg[:, g % 2, :].rearrange("p (h d) -> p h d", d=64),
                    eQb[:, s2, :].rearrange("p (h d) -> p h d", d=64),
                    kg[:, s2, :].unsqueeze(1).broadcast_to([128, 16, 64]),
                    op=ALU.mult).then_inc(s_dg, 1)

            def cevac_item(t, ci, s):
                q, oh = divmod(ci, 2)
                ve.wait_ge(s_cp, (t * NQ + q) * 8 + oh * 4 + s + 1)
                half = (pQ0, pQ1, pK0, pK1)[s][:, :]
                dst = partial[:, s, oh * 512:(oh + 1) * 512]
                if q == 0:
                    if t > 0:
                        ve.wait_ge(s_out, 64 * t)
                    ve.tensor_add(dst, half,
                                  x_sb[:, s, oh * 512:(oh + 1) * 512]
                                  ).then_inc(s_ce, 1)
                elif q < NQ - 1:
                    ve.tensor_add(dst, half, dst).then_inc(s_ce, 1)
                else:
                    ve.tensor_add(dst, half, dst).then_inc(s_ce, 1)
                    ve.tensor_add(dst, dst,
                                  attn[:, s, oh * 512:(oh + 1) * 512]
                                  ).then_inc(s_fin, 1)

            def mstep(t, s, items=None):
                g = g_of(t, s)
                s2 = g % 2
                for h in range(NDV):
                    ve.tensor_tensor(
                        prodM[:, 0, :].rearrange("p (a d) -> p a d", a=16),
                        eQg[:, g % 2, h * 64:(h + 1) * 64].unsqueeze(1)
                            .broadcast_to([128, 16, 64]),
                        eKb[:, g % 4, :].rearrange("p (a d) -> p a d", a=16),
                        op=ALU.mult)
                    ve.reduce_sum(Mf[:, h * 16:(h + 1) * 16],
                                  prodM[:, 0, :].rearrange("p (a d) -> p a d",
                                                           a=16), axis=AX.X)
                for j in range(NGP):
                    h = NDV + j
                    ve.wait_ge(s_gm, NGP * g + j + 1)
                    ve.reduce_sum(Mf[:, h * 16:(h + 1) * 16],
                                  prodM[:, 1 + (j % 2), :].rearrange(
                                      "p (a d) -> p a d", a=16),
                                  axis=AX.X).then_inc(s_mr, 1)
                    if items and j >= 2:
                        cevac_item(t, *items.pop(0))
                if g >= 2:
                    ve.wait_ge(s_go, (16 - ODV) * (g - 1))
                ve.wait_ge(s_mr, NGP * (g + 1))
                ve.tensor_tensor(
                    Mb[:, s2, :].rearrange("p (h i) -> p h i", h=16),
                    Mf[:, :].rearrange("p (h i) -> p h i", h=16),
                    rq[:, s2, :].unsqueeze(2).broadcast_to([128, 16, 16]),
                    op=ALU.mult).then_inc(s_dM, 1)

            def outred(t, s, items=None):
                g = g_of(t, s)
                s3 = g % 4
                ve.wait_ge(s_dM, g + 1)
                ve.wait_ge(s_v, g + 1)

                def dmult(h):
                    ve.tensor_tensor(
                        prodO[:, h % 2, :].rearrange("p (a e) -> p a e", a=16),
                        Mb[:, g % 2, h * 16:(h + 1) * 16].unsqueeze(2)
                            .broadcast_to([128, 16, 64]),
                        Vb[:, s3, :].rearrange("p (a e) -> p a e", a=16),
                        op=ALU.mult)

                def dred(h):
                    if h >= ODV:
                        ve.wait_ge(s_go, (16 - ODV) * g + (h - ODV) + 1)
                    with nc.allow_low_precision(
                            reason="DVE reduce accumulates fp32 internally; "
                                   "bf16 only on the final write"):
                        ve.reduce_sum(attn[:, s, h * 64:(h + 1) * 64],
                                      prodO[:, h % 2, :].rearrange(
                                          "p (a e) -> p e a", a=16),
                                      axis=AX.X).then_inc(s_or, 1)
                    if items and h >= ODV + 2:
                        cevac_item(t, *items.pop(0))

                dmult(0)
                dmult(1)
                for h in range(ODV - 2):
                    dred(h)
                    dmult(h + 2)
                dred(ODV - 2)
                dred(ODV - 1)
                for h in range(ODV, 16):
                    dred(h)

            def main_tile(t):
                items = [(ci, s) for ci in range(12) for s in range(SUBS)]
                mstep(t, 0)
                outred(t, 0, items)
                mstep(t, 1, items)
                outred(t, 1, items)
                prep(t, 2); mstep(t, 2, items)
                outred(t, 2, items)
                if t + 1 < NT:
                    ln_stats(t + 1, 0); ln_stats(t + 1, 1)
                prep(t, 3); mstep(t, 3, items)
                outred(t, 3, items)
                for ci, s in items:
                    cevac_item(t, ci, s)
                for s in range(SUBS):
                    cevac_item(t, 12, s)
                for s in range(SUBS):
                    cevac_item(t, 13, s)
                if t + 1 < NT:
                    ln_stats(t + 1, 2); ln_stats(t + 1, 3)
                for s in range(SUBS):
                    cevac_item(t, 14, s)
                if t + 1 < NT:
                    ln_apply(t + 1, 0); ln_apply(t + 1, 1)
                for s in range(SUBS):
                    cevac_item(t, 15, s)

            for hc in range(32):
                ve.wait_ge(s_tA, hc + 1)
                ve.tensor_copy(bdrm[:, :], pT[0:32, 0:128])
                ve.tensor_copy(rupS[:, :], pT[0:32, 128:384]).then_inc(s_eA, 1)
                ve.wait_ge(s_cm, hc + 1)
                if hc >= 2:
                    ve.wait_ge((s_zd0, s_zd1)[hc % 2], 16 * (hc // 2))
                ins = None
                for p in range(4):
                    ve.tensor_scalar(
                        out=zstg[:, hc % 2, p, 0:64],
                        in0=pQ0[0:32, p * 64:(p + 1) * 64],
                        scalar1=msk[:, 0:1], scalar2=None, op0=ALU.mult)
                    ins = ve.tensor_scalar(
                        out=zstg[:, hc % 2, p, 64:128],
                        in0=pQ0[0:32, p * 64:(p + 1) * 64],
                        scalar1=msk[:, 1:2], scalar2=None, op0=ALU.mult)
                ins.then_inc(s_mk, 1)

            ln_chain(0, 0); ln_chain(0, 1); evacT(0, 0)
            ln_chain(0, 2); evacT(0, 1)
            prep(0, 0)
            ln_chain(0, 3); evacT(0, 2)
            prep(0, 1)
            evacT(0, 3)
            main_tile(0)
            for t in range(1, NT):
                evacT(t, 0); evacT(t, 1)
                ln_apply(t, 2); ln_apply(t, 3)
                prep(t, 0)
                evacT(t, 2)
                prep(t, 1)
                evacT(t, 3)
                main_tile(t)

    return nc


def kernel(L0_attn_input_seq_embs, L0_attn_W_Q, L0_attn_W_K, L0_attn_W_V,
           L0_ffn_rail_maker, L0_ffn_rail_up_proj, L0_ffn_w_cproj):
    x = np.ascontiguousarray(np.asarray(L0_attn_input_seq_embs, dtype=np.float32))
    wq = np.ascontiguousarray(np.asarray(L0_attn_W_Q, dtype=np.float32))
    wk = np.ascontiguousarray(np.asarray(L0_attn_W_K, dtype=np.float32))
    wv = np.ascontiguousarray(np.asarray(L0_attn_W_V, dtype=np.float32))
    rm = np.ascontiguousarray(np.asarray(L0_ffn_rail_maker, dtype=np.float32))
    rup = np.ascontiguousarray(np.asarray(L0_ffn_rail_up_proj, dtype=np.float32))
    wc = np.ascontiguousarray(np.asarray(L0_ffn_w_cproj, dtype=np.float32))

    nc = build_core_program(L=x.shape[1])
    in_maps = [{"x": np.ascontiguousarray(x[b]), "wq": wq, "wk": wk, "wv": wv,
                "rm": rm, "rup": rup, "wc": wc} for b in range(x.shape[0])]
    res = run_bass_kernel_spmd(nc, in_maps, core_ids=list(range(x.shape[0])))
    return np.stack([np.asarray(r["out"], dtype=np.float32) for r in res.results],
                    axis=0)
